# version 29
# baseline (speedup 1.0000x reference)
import numpy as np
import ml_dtypes
import concourse.bass as bass
import concourse.mybir as mybir
from concourse.bass_utils import run_bass_kernel_spmd
from concourse.aot_env import allow_ici_all_to_all

F32 = mybir.dt.float32
BF16 = mybir.dt.bfloat16
U8 = mybir.dt.uint8
AF = mybir.ActivationFunctionType
ALU = mybir.AluOpType
AP = bass.AP

NCORES = 8
H = 32
HPC = 4
D = 4096
S = 8192
T = 1024
CH = 512
NCHUNK = 16
QKSCALE = float(192.0 ** -0.5)
EXPBIAS = -12.0
NE = 3277
EPS = 1e-5

DEBUG_TAPS = False


def ppos(g):
    return 2 * g if g <= 7 else 2 * (15 - g) + 1


class CEng:

    def __init__(self, sem, self_chain=True):
        self.sem, self.n, self.e, self.lastw = sem, 0, None, {}
        self.self_chain = self_chain

    def op(self, fname, *args, waits=(), inc=True, **kw):
        if inc:
            self.n += 1
        if self.e is not None:
            if self.self_chain and inc and self.n > 1:
                self.e.wait_ge(self.sem, self.n - 1)
            for ev in waits:
                if ev is None:
                    continue
                sem, cnt = ev
                if sem is self.sem or cnt <= 0:
                    continue
                if cnt > self.lastw.get(sem.name, 0):
                    self.e.wait_ge(sem, cnt)
                    self.lastw[sem.name] = cnt
            ins = getattr(self.e, fname)(*args, **kw)
            if inc:
                ins.then_inc(self.sem, 1)
        return (self.sem, self.n)

    @property
    def now(self):
        return (self.sem, self.n)


class QEng:

    def __init__(self, sem):
        self.sem, self.n, self.e, self.lastw = sem, 0, None, {}

    def _waits(self, waits):
        for ev in waits:
            if ev is None:
                continue
            sem, cnt = ev
            if cnt <= 0:
                continue
            if cnt > self.lastw.get(sem.name, 0):
                self.e.wait_ge(sem, cnt)
                self.lastw[sem.name] = cnt

    def dma(self, out, in_, waits=(), sem2=None, **kw):
        if sem2 is not None:
            sem, cnt = sem2
            cnt[0] += 16
            if self.e is not None:
                self._waits(waits)
                self.e.dma_start(out=out, in_=in_, **kw).then_inc(sem, 16)
            return (sem, cnt[0])
        self.n += 16
        if self.e is not None:
            self._waits(waits)
            self.e.dma_start(out=out, in_=in_, **kw).then_inc(self.sem, 16)
        return (self.sem, self.n)

    def cc(self, kind, rg, ins, outs, ccsem, waits=()):
        if self.e is not None:
            self._waits(waits)
            with allow_ici_all_to_all():
                self.e.collective_compute(
                    kind, ALU.bypass, replica_groups=rg, ins=ins, outs=outs,
                ).then_inc(ccsem, 1)
        return (ccsem, 1)

    def wait(self, waits):
        if self.e is not None:
            self._waits(waits)

    @property
    def now(self):
        return (self.sem, self.n)


def _host_consts():
    ident = np.eye(128, dtype=ml_dtypes.bfloat16)
    masks = np.zeros((128, 4 * CH), dtype=ml_dtypes.bfloat16)
    kk = np.arange(128)[:, None]
    qq = np.arange(CH)[None, :]
    for o in range(4):
        masks[:, o * CH:(o + 1) * CH] = ((o * 128 + kk) <= qq).astype(
            ml_dtypes.bfloat16)
    cvec = np.empty((128, 32), dtype=np.float32)
    for t in range(32):
        d = t * 128 + np.arange(128)
        cvec[:, t] = np.where(d < NE, 2.0, -8.0)
    return ident, masks, cvec


def build_nc():
    nc = bass.Bass("TRN2", target_bir_lowering=False, debug=False)

    x_my = nc.dram_tensor("x_my", [T, D], F32, kind="ExternalInput").ap()
    wq_t = nc.dram_tensor("wq_t", [16 * 128, 32 * 256], F32,
                          kind="ExternalInput").ap()
    wk_t = nc.dram_tensor("wk_t", [16 * 128, 32 * 256], F32,
                          kind="ExternalInput").ap()
    wv_t = nc.dram_tensor("wv_t", [16 * 128, 32 * 256], F32,
                          kind="ExternalInput").ap()
    wo_in = nc.dram_tensor("wo_in", [16 * 128, 32 * 256], F32,
                           kind="ExternalInput").ap()
    km_in = [nc.dram_tensor(f"k{i}_in", [16 * 128, 32 * 256], U8,
                            kind="ExternalInput").ap() for i in range(3)]
    scal3 = nc.dram_tensor("scal3", [1, 3], F32, kind="ExternalInput").ap()
    y_my = nc.dram_tensor("y_my", [T, D], F32, kind="ExternalOutput").ap()

    cc_q_in = nc.dram_tensor("cc_q_in", [NCORES * 128, HPC * T], BF16)
    cc_q_out = nc.dram_tensor("cc_q_out", [NCORES * 128, HPC * T], BF16)
    cc_k_in = nc.dram_tensor("cc_k_in", [NCORES * 128, HPC * T], BF16)
    cc_k_out = nc.dram_tensor("cc_k_out", [NCORES * 128, HPC * T], BF16)
    cc_vin2 = nc.dram_tensor("cc_vin2", [NCORES * 128, HPC * T], BF16)
    cc_vout2 = nc.dram_tensor("cc_vout2", [NCORES * 128, HPC * T], BF16)
    cc_at_in = [nc.dram_tensor(f"cc_at_in{i}", [NCORES * 128, 2 * T], BF16)
                for i in range(2)]
    cc_at_out = [nc.dram_tensor(f"cc_at_out{i}", [NCORES * 128, 2 * T], BF16)
                 for i in range(2)]
    attn_tok_d = nc.dram_tensor("attn_tok_d", [128, 8 * D], BF16)
    x1_d = nc.dram_tensor("x1_d", [128, 8 * D], F32)
    ffn_tok_d = nc.dram_tensor("ffn_tok_d", [128, 8 * D], BF16)

    ident_np, masks_np, cvec_np = _host_consts()
    ident_dr = nc.inline_tensor(ident_np, "ident")
    masks_dr = nc.inline_tensor(masks_np, "masks")
    cvec_dr = nc.inline_tensor(cvec_np, "cvec")

    taps = {}
    if DEBUG_TAPS:
        taps["agout"] = nc.dram_tensor("tap_agout", [NCORES * 128, 32 * T],
                                       BF16, kind="ExternalOutput").ap()
        taps["qt"] = nc.dram_tensor("tap_qt", [128, HPC * S], BF16,
                                    kind="ExternalOutput").ap()
        taps["kt"] = nc.dram_tensor("tap_kt", [128, HPC * S], BF16,
                                    kind="ExternalOutput").ap()
        taps["v"] = nc.dram_tensor("tap_v", [128, HPC * S], BF16,
                                   kind="ExternalOutput").ap()
        taps["attnT"] = nc.dram_tensor("tap_attnT", [128, HPC * S], BF16,
                                       kind="ExternalOutput").ap()
        taps["atout"] = nc.dram_tensor("tap_atout", [NCORES * 128, HPC * T],
                                       BF16, kind="ExternalOutput").ap()
        taps["x1"] = nc.dram_tensor("tap_x1", [128, 8 * D], F32,
                                    kind="ExternalOutput").ap()

    BASE = 16512
    bump = [BASE]

    def at(name, shape, dtype, region_off=None):
        o = bump[0] if region_off is None else region_off
        t = nc.alloc_sbuf_tensor_at(name, shape, dtype, offset=o)
        if region_off is None:
            nbytes = int(np.prod(shape[1:])) * mybir.dt.size(dtype)
            bump[0] = (o + nbytes + 31) & ~31
        return t

    s_ident = at("s_ident", [128, 128], BF16)
    s_masks = at("s_masks", [128, 4 * CH], BF16)
    s_cvec = at("s_cvec", [128, 32], F32)
    s_expb = at("s_expb", [128, 1], F32)
    s_ones_c_f32 = at("s_ones_c_f32", [128, 1], F32)
    s_ones_c_bf = at("s_ones_c_bf", [128, 1], BF16)
    s_ones_r_f32 = at("s_ones_r_f32", [1, 128], F32)
    s_ones_mat = at("s_ones_mat", [128, 128], BF16)
    s_negh = at("s_negh", [1, 128], F32)
    s_sc3 = at("s_sc3", [1, 3], F32)
    s_sv = at("s_sv", [128, 4], F32)
    s_sv2 = at("s_sv2", [128, 4], F32)
    s_ptp = [at(f"s_ptp{i}", [128, 2 * CH], BF16) for i in range(4)]
    s_ptm = [at(f"s_ptm{i}", [128, 2 * CH], BF16) for i in range(2)]
    s_recb = [at(f"s_recb{i}", [128, CH], F32) for i in range(2)]
    s_S = at("s_S", [1, 2 * CH], F32)
    s_act = [at(f"s_act{i}", [128, CH], BF16) for i in range(2)]
    s_stg = [at(f"s_stg{i}", [128, CH], BF16) for i in range(2)]
    s_ln = at("s_ln", [128, 8], F32)
    s_bnst = at("s_bnst", [128, 48], F32)
    A0 = (bump[0] + 31) & ~31
    B0 = A0 + 64 * 1024
    C0 = B0 + 64 * 1024
    E0 = C0 + 32 * 1024
    assert E0 + 16 * 1024 <= 229344, (A0, E0)

    s_xt = at("s_xt", [128, 32 * T], BF16, A0)
    s_agh = [at(f"s_agh{i}", [128, 32 * 512], BF16, A0 + i * 32 * 1024)
             for i in range(2)]
    s_kh = at("s_kh", [128, S], BF16, A0)
    s_vh = at("s_vh", [128, S], BF16, A0 + 16 * 1024)
    s_qh = at("s_qh", [128, S], BF16, A0 + 32 * 1024)
    s_xt2 = at("s_xt2", [128, 32 * T], BF16, A0)
    s_xx2 = at("s_xx2", [128, 32 * T], BF16, A0)

    s_wq = at("s_wq", [128, 32 * 512], BF16, B0)
    s_wk = at("s_wk", [128, 32 * 512], BF16, B0 + 32 * 1024)
    s_attnT = at("s_attnT", [128, HPC * S], BF16, B0)
    s_attnF = at("s_attnF", [128, 32 * T], BF16, B0)
    s_xx1 = at("s_xx1", [128, 32 * T], BF16, B0)
    s_x0t = at("s_x0t", [128, D], F32, B0)
    s_att = at("s_att", [128, D], BF16, B0 + 16 * 1024)
    s_x1t = at("s_x1t", [128, D], F32, B0 + 24 * 1024)
    s_fft = at("s_fft", [128, D], BF16, B0 + 40 * 1024)
    s_yt = at("s_yt", [128, D], F32, B0 + 48 * 1024)

    s_wv = at("s_wv", [128, 32 * 512], BF16, C0)
    s_wring = [at(f"s_wring{i}", [128, 32 * 256], BF16, C0 + i * 16 * 1024)
               for i in range(2)]

    s_x0 = at("s_x0", [128, D], F32, E0)
    s_dnb = [at(f"s_dnb{i}", [128, CH], F32, E0 + i * 2048) for i in range(2)]

    import contextlib
    pctx = contextlib.ExitStack()
    psQ = [pctx.enter_context(nc.psum_tensor(f"psQ{i}", [128, 1024], F32))
           for i in range(2)]
    psC = pctx.enter_context(nc.psum_tensor("psC", [128, 512], F32))
    psD = pctx.enter_context(nc.psum_tensor("psD", [128, 512], F32))
    psE = [pctx.enter_context(nc.psum_tensor(f"psE{i}", [128, 512], F32))
           for i in range(2)]
    ps4 = [psQ[0][:, 0:512], psQ[0][:, 512:1024],
           psQ[1][:, 0:512], psQ[1][:, 512:1024]]
    psCD_bf = [psC[:, :].bitcast(BF16), psD[:, :].bitcast(BF16)]

    sctx = contextlib.ExitStack()
    sem = {k: sctx.enter_context(nc.semaphore(k))
           for k in ["pe", "ve", "se", "sy", "gp", "ccq", "cck", "ccv",
                     "ccat", "ccat2", "stgq0", "stgq1", "stgk0", "stgk1",
                     "stgv0", "stgv1", "stgs0", "stgs1", "stg",
                     "qx0", "qat", "qsp", "qff", "qxl"]}
    rg = [list(range(NCORES))]

    def build_all(live):
        PE = CEng(sem["pe"], self_chain=False)
        V, SC = CEng(sem["ve"]), CEng(sem["se"])
        stg2 = (sem["stg"], [0])
        q_x0t = (sem["qx0"], [0])
        q_att = (sem["qat"], [0])
        q_sp = (sem["qsp"], [0])
        q_ff = (sem["qff"], [0])
        q_xl = (sem["qxl"], [0])
        SY, G = QEng(sem["sy"]), QEng(sem["gp"])
        PE.e = live.get("tensor")
        V.e = live.get("vector")
        SC.e = live.get("scalar")
        SY.e = live.get("sync")
        G.e = live.get("gpsimd")

        e_id = SY.dma(s_ident[:], ident_dr.ap())
        e_mk = SY.dma(s_masks[:], masks_dr.ap())
        e_cv = SY.dma(s_cvec[:], cvec_dr.ap())
        e_s3 = SY.dma(s_sc3[:], scal3)
        e_exb = V.op("memset", s_expb[:], EXPBIAS)
        e_o1 = V.op("memset", s_ones_c_f32[:], 1.0)
        e_o2 = V.op("memset", s_ones_c_bf[:], 1.0)
        e_o3 = V.op("memset", s_ones_r_f32[:], 1.0)
        e_nh = V.op("memset", s_negh[:], -0.5)
        e_om = V.op("memset", s_ones_mat[:], 1.0)
        e_bs = PE.op("matmul", psE[0][:, 0:3], s_ones_r_f32[:, :],
                     s_sc3[:, :], start=True, stop=True, waits=[e_o3, e_s3])
        e_sv = SC.op("activation", s_sv[:, 0:3], psE[0][:, 0:3], AF.Copy,
                     waits=[e_bs])
        e_sv2 = V.op("tensor_scalar_mul", s_sv2[:, 0:3], s_sv[:, 0:3], 2.0,
                     waits=[e_sv])


        e_stg_free = [None, None]
        e_tr_free = [None, None]
        prev_x0_reader = None
        for tt in range(8):
            w = [prev_x0_reader] if prev_x0_reader else []
            e_x0 = SY.dma(s_x0[:], x_my[tt * 128:(tt + 1) * 128, :], waits=w)
            e_bn = None
            for i in range(8):
                e_bn = V.op("bn_stats", s_bnst[:, i * 6:(i + 1) * 6],
                            s_x0[:, i * 512:(i + 1) * 512],
                            waits=[e_x0] if i == 0 else [])
            e_ag2 = V.op("bn_aggr", s_ln[:, 0:2], s_bnst[:])
            e_vp = V.op("tensor_scalar_add", s_ln[:, 2:3], s_ln[:, 1:2], EPS)
            e_rc = V.op("reciprocal", s_ln[:, 3:4], s_ln[:, 2:3])
            e_sq = SC.op("activation", s_ln[:, 4:5], s_ln[:, 3:4], AF.Sqrt,
                         waits=[e_rc])
            e_nm = V.op("tensor_scalar_mul", s_ln[:, 5:6], s_ln[:, 0:1], -1.0)
            last_apply = None
            for q in range(8):
                sl = q % 2
                w = [e_nm, e_sq]
                if e_stg_free[sl]:
                    w.append(e_stg_free[sl])
                e_ap = V.op("tensor_scalar", s_stg[sl][:],
                            s_x0[:, q * 512:(q + 1) * 512],
                            s_ln[:, 5:6], s_ln[:, 4:5], ALU.add, ALU.mult,
                            waits=w)
                last_apply = e_ap
                pt = psCD_bf[sl]
                w = [e_ap, e_id]
                if e_tr_free[sl]:
                    w.append(e_tr_free[sl])
                etr = None
                for b in range(4):
                    etr = PE.op("transpose", pt[:, b * 128:(b + 1) * 128],
                                s_stg[sl][:, b * 128:(b + 1) * 128],
                                s_ident[:], waits=w if b == 0 else [],
                                inc=(b == 3))
                e_stg_free[sl] = etr
                _b = s_xt[:]
                dst = AP(_b.tensor, _b.offset + q * 4 * 1024 + tt * 128,
                         [[32 * T, 128], [1024, 4], [1, 128]])
                e_ev = SC.op("activation", dst,
                             pt[:, 0:512].rearrange("p (b c) -> p b c", c=128),
                             AF.Copy, waits=[etr])
                e_tr_free[sl] = e_ev
            prev_x0_reader = last_apply
            if tt == 3:
                e_half0 = [e for e in e_tr_free if e]
        e_half1 = [e for e in e_tr_free if e]

        pev = [e_sv, None, None, None]
        e_stg2_free = [None, None]
        ring_free = [None, None]
        ring_ld = {}
        rblk = 0
        pslot = 0
        e_cc = {}
        e_last_mm = None

        def ring_load(wsrc, eb, waits_extra=()):
            nonlocal rblk
            rb = s_wring[rblk % 2]
            w = list(waits_extra)
            if ring_free[rblk % 2]:
                w.append(ring_free[rblk % 2])
            ev = G.dma(rb[:], wsrc[eb * 128:(eb + 1) * 128, :], waits=w)
            ring_ld[rblk] = ev
            rblk += 1
            return rb, ev, (rblk - 1) % 2

        stgs = {("q", 0): (sem["stgq0"], [0]), ("q", 1): (sem["stgq1"], [0]),
                ("k", 0): (sem["stgk0"], [0]), ("k", 1): (sem["stgk1"], [0]),
                ("v", 0): (sem["stgv0"], [0]), ("v", 1): (sem["stgv1"], [0])}
        for mat, wsrc, ccin, ccsem, ccout in (
                ("q", wq_t, cc_q_in, "ccq", cc_q_out),
                ("k", wk_t, cc_k_in, "cck", cc_k_out),
                ("v", wv_t, cc_vin2, "ccv", cc_vout2)):
            e_stg_last = None
            if mat == "q":
                loop = [(th, eb) for th in range(2) for eb in range(16)]
            else:
                loop = [(th, eb) for eb in range(16) for th in range(2)]
            last_eb = None
            for th, eb in loop:
                if eb != last_eb or (mat == "q"):
                    rb, e_w, rslot = ring_load(wsrc, eb)
                    last_eb = eb
                last_use = None
                for et in range(2):
                    ps = ps4[pslot % 4]
                    wts = [e_w]
                    wts += e_half0 if th == 0 else e_half1
                    if pev[pslot % 4]:
                        wts.append(pev[pslot % 4])
                    e_mm = None
                    for dt in range(32):
                        if mat != "v":
                            e_mm = PE.op(
                                "matmul", ps,
                                rb[:, dt * 256 + et * 128:
                                   dt * 256 + et * 128 + 128],
                                s_xt[:, dt * T + th * 512:
                                     dt * T + th * 512 + 512],
                                start=(dt == 0), stop=(dt == 31),
                                waits=wts if dt == 0 else [],
                                inc=(dt == 31))
                        else:
                            pass
                    if mat == "v":
                        break
                    last_use = e_mm
                    aslot = pslot % 2
                    w2 = [e_mm]
                    if e_stg2_free[aslot]:
                        w2.append(e_stg2_free[aslot])
                    e_ev = SC.op("activation", s_act[aslot][:], ps, AF.Copy,
                                 waits=w2)
                    pev[pslot % 4] = e_ev
                    h = 2 * eb + et
                    d, hlh = h // 4, h % 4
                    _bi = ccin.ap()
                    dst = AP(_bi.tensor,
                             _bi.offset + d * 128 * (HPC * T)
                             + hlh * T + th * 512,
                             [[HPC * T, 128], [1, 512]])
                    e_st = SY.dma(dst, s_act[aslot][:], waits=[e_ev],
                                  sem2=stgs[(mat, aslot)])
                    e_stg2_free[aslot] = e_st
                    e_stg_last = e_st
                    pslot += 1
                if mat == "v":
                    for sub in range(4):
                        tt4 = th * 4 + sub
                        ps = ps4[pslot % 4]
                        wts = [e_w]
                        wts += e_half0 if th == 0 else e_half1
                        if pev[pslot % 4]:
                            wts.append(pev[pslot % 4])
                        e_mm = None
                        for dt in range(32):
                            e_mm = PE.op(
                                "matmul", ps[:, 0:256],
                                s_xt[:, dt * T + tt4 * 128:
                                     dt * T + tt4 * 128 + 128],
                                rb[:, dt * 256:(dt + 1) * 256],
                                start=(dt == 0), stop=(dt == 31),
                                waits=wts if dt == 0 else [],
                                inc=(dt == 31))
                        last_use = e_mm
                        aslot = pslot % 2
                        w2 = [e_mm]
                        if e_stg2_free[aslot]:
                            w2.append(e_stg2_free[aslot])
                        e_ev = SC.op("activation", s_act[aslot][:, 0:256],
                                     ps[:, 0:256], AF.Copy, waits=w2)
                        pev[pslot % 4] = e_ev
                        d, b2 = eb // 2, eb % 2
                        _bi = ccin.ap()
                        dst = AP(_bi.tensor,
                                 _bi.offset + d * 128 * (HPC * T)
                                 + tt4 * 512 + b2 * 256,
                                 [[HPC * T, 128], [1, 256]])
                        e_st = SY.dma(dst, s_act[aslot][:, 0:256],
                                      waits=[e_ev], sem2=stgs[(mat, aslot)])
                        e_stg2_free[aslot] = e_st
                        e_stg_last = e_st
                        pslot += 1
                ring_free[rslot] = last_use
                e_last_mm = last_use
            e_cc[mat] = G.cc(
                "AllToAll", rg, [ccin.ap().opt()], [ccout.ap().opt()],
                sem[ccsem],
                waits=[(stgs[(mat, 0)][0], stgs[(mat, 0)][1][0]),
                       (stgs[(mat, 1)][0], stgs[(mat, 1)][1][0])])
        e_qkv_cc = [e_cc["q"], e_cc["k"], e_cc["v"]]
        e_v_last_mm = e_last_mm

        wo_ld = {}
        for eb in range(2):
            rb, ev, rslot = ring_load(wo_in, eb)
            wo_ld[eb] = (rb, ev)
            ring_free[rslot] = None

        pr_glob = 0
        e_exp_by_par = [None, None]
        e_ptp_free = [[], [], [], []]
        e_ptm_free = [[], []]
        e_attnevac = [e_tr_free[0], e_tr_free[1]]
        e_recb_free = [None, None]
        e_dnb_free = [None, None]
        e_denfree = [e_sv, None]
        e_atst = [None, None]
        e_at = [None, None]
        e_head_last_use = None
        e_head_dma = None
        for hl in range(HPC):
            w = list(e_qkv_cc) + [e_v_last_mm]
            if e_head_last_use:
                w.append(e_head_last_use)
            _k = cc_k_out.ap()
            SY.dma(s_kh[:].rearrange("p (r t) -> p r t", t=T),
                   AP(_k.tensor, _k.offset + hl * T,
                      [[HPC * T, 128], [128 * HPC * T, 8], [1, T]]),
                   waits=w)
            _v = cc_vout2.ap()
            for r in range(8):
                SY.dma(s_vh[:, r * T:(r + 1) * T].rearrange(
                           "p (tt c) -> p tt c", c=128),
                       AP(_v.tensor, _v.offset + r * 128 * (HPC * T)
                          + hl * 128,
                          [[HPC * T, 128], [512, 8], [1, 128]]))
            _q = cc_q_out.ap()
            e_head_dma = SY.dma(
                s_qh[:].rearrange("p (r t) -> p r t", t=T),
                AP(_q.tensor, _q.offset + hl * T,
                   [[HPC * T, 128], [128 * HPC * T, 8], [1, T]]))
            if hl == 3:
                e_at[0] = G.cc("AllToAll", rg, [cc_at_in[0].ap().opt()],
                               [cc_at_out[0].ap().opt()], sem["ccat"],
                               waits=[e_atst[0], e_head_dma])
            head_last = None
            pend = None
            e_pv_done_by_jp = [None, None]

            def flush_pend():
                nonlocal pend, head_last
                if pend is None:
                    return
                (pfin, e_fin, kcols, psAT_, jp_, m0, kmax_,
                 ptsl_, diag_, ptm_) = pend
                pend = None
                e_last = None
                for half in range(2):
                    m = m0 + half
                    w = [e_fin]
                    if m == 0 and e_attnevac[jp_]:
                        w.append(e_attnevac[jp_])
                    PE.op("matmul", psAT_[:],
                          s_vh[:, kcols[half]:kcols[half] + 128],
                          pfin[:, half * 512:(half + 1) * 512],
                          start=(m == 0), stop=(m == kmax_ - 1),
                          waits=w if half == 0 else [], inc=False)
                    wd = []
                    if m == 0:
                        wd.append(e_om)
                        if e_denfree[jp_]:
                            wd.append(e_denfree[jp_])
                    e_last = PE.op("matmul", psE[jp_][:],
                                   s_ones_mat[:, :],
                                   pfin[:, half * 512:(half + 1) * 512],
                                   start=(m == 0), stop=(m == kmax_ - 1),
                                   waits=wd, inc=(half == 1))
                head_last = e_last
                if diag_:
                    e_ptm_free[ptm_] = [e_last]
                else:
                    e_ptp_free[ptsl_] = [e_last]
                e_pv_done_by_jp[jp_] = e_last
            for j in range(NCHUNK):
                jp = j % 2
                qcol = ppos(j) * CH
                psAT = psC if jp == 0 else psD
                kmax = 4 * (j + 1)
                for pr in range(kmax // 2):
                    par = pr_glob % 2
                    psq = psQ[par]
                    ptsl = pr_glob % 4
                    e_sc = None
                    for half in range(2):
                        m = 2 * pr + half
                        g, o = m // 4, m % 4
                        kcol = ppos(g) * CH + o * 128
                        w = [e_head_dma]
                        if e_exp_by_par[par]:
                            w.append(e_exp_by_par[par])
                        e_sc = PE.op("matmul",
                                     psq[:, half * 512:(half + 1) * 512],
                                     s_kh[:, kcol:kcol + 128],
                                     s_qh[:, qcol:qcol + CH],
                                     start=True, stop=True,
                                     waits=w if half == 0 else [],
                                     inc=(half == 1))
                    flush_pend()
                    w = [e_sc, e_exb] + e_ptp_free[ptsl]
                    e_exp = SC.op("activation", s_ptp[ptsl][:], psq[:, :],
                                  AF.Exp, bias=s_expb[:, :], scale=QKSCALE,
                                  waits=w)
                    e_exp_by_par[par] = e_exp
                    diag = ((2 * pr) // 4 == j)
                    pfin, e_fin = s_ptp[ptsl], e_exp
                    if diag:
                        o0 = (2 * pr) % 4
                        e_fin = V.op("tensor_mul", s_ptm[pr % 2][:],
                                     s_ptp[ptsl][:],
                                     s_masks[:, o0 * CH:(o0 + 2) * CH],
                                     waits=[e_exp, e_mk] + e_ptm_free[pr % 2])
                        pfin = s_ptm[pr % 2]
                    kcols = (ppos((2 * pr) // 4) * CH + ((2 * pr) % 4) * 128,
                             ppos((2 * pr + 1) // 4) * CH
                             + ((2 * pr + 1) % 4) * 128)
                    if diag:
                        e_ptp_free[ptsl] = [e_fin]
                    pend = (pfin, e_fin, kcols, psAT, jp, 2 * pr, kmax,
                            ptsl, diag, pr % 2)
                    pr_glob += 1
                flush_pend()
                e_pv_done = e_pv_done_by_jp[jp]
                w = [e_pv_done]
                if e_dnb_free[jp]:
                    w.append(e_dnb_free[jp])
                e_dev = SC.op("activation", s_dnb[jp][:], psE[jp][:],
                              AF.Copy, waits=w)
                e_denfree[jp] = e_dev
                w = [e_dev]
                if e_recb_free[jp]:
                    w.append(e_recb_free[jp])
                e_rcp = V.op("reciprocal", s_recb[jp][:], s_dnb[jp][:],
                             waits=w)
                e_dnb_free[jp] = e_rcp
                e_aev = V.op("tensor_mul",
                             s_attnT[:, hl * S + qcol:hl * S + qcol + CH],
                             psAT[:], s_recb[jp][:],
                             waits=[e_pv_done, e_rcp])
                e_attnevac[jp] = e_aev
                e_recb_free[jp] = e_aev
            if hl == 1 or hl == 3:
                part = hl // 2
                w = [e for e in e_attnevac if e]
                _bi = cc_at_in[part].ap()
                srcd = s_attnT[:].rearrange("p (hl t8) -> p hl t8", t8=S)
                for d in range(8):
                    dst = AP(_bi.tensor, _bi.offset + d * 128 * (2 * T),
                             [[2 * T, 128], [T, 2], [1, T]])
                    e_atst[part] = SY.dma(
                        dst,
                        srcd[:, 2 * part:2 * part + 2, d * T:(d + 1) * T],
                        waits=w if d == 0 else [], sem2=stg2)
                if part == 1:
                    e_at[1] = G.cc("AllToAll", rg, [cc_at_in[1].ap().opt()],
                                   [cc_at_out[1].ap().opt()], sem["ccat2"],
                                   waits=[e_atst[1]])
            e_head_last_use = head_last

        if DEBUG_TAPS:
            SY.dma(taps["attnT"], s_attnT[:],
                   waits=[e for e in e_attnevac if e])
            for part in range(2):
                SY.dma(taps["atout"][:, part * 2 * T:(part + 1) * 2 * T],
                       cc_at_out[part].ap(), waits=[e_at[part]])
        e_atF = None
        for part in range(2):
            _bo = cc_at_out[part].ap()
            _bf = s_attnF[:]
            for r in range(8):
                srcr = AP(_bo.tensor, _bo.offset + r * 128 * (2 * T),
                          [[2 * T, 128], [T, 2], [1, T]])
                dstr = AP(_bf.tensor, _bf.offset + (4 * r + 2 * part) * T,
                          [[32 * T, 128], [T, 2], [1, T]])
                e_atF = SY.dma(dstr, srcr,
                               waits=[e_at[0], e_at[1], e_atst[1]]
                               if r == 0 else [])

        pev4 = [e_exp_by_par[0], e_exp_by_par[0],
                e_exp_by_par[1], e_exp_by_par[1]]
        e_tr_free = [e_attnevac[0], e_attnevac[1]]
        pslot = 0
        trslot = 0
        pend4 = None
        st_last = [None]
        stgs_st = [(sem["stgs0"], [0]), (sem["stgs1"], [0])]

        def flush4(dram_t):
            nonlocal pend4, trslot
            if pend4 is None:
                return
            (aslot, e_ev, th_, eb_, et_) = pend4
            pend4 = None
            tsl = trslot % 2
            pt = psCD_bf[tsl]
            wtr = [e_ev]
            if e_tr_free[tsl]:
                wtr.append(e_tr_free[tsl])
            etr = None
            for bb in range(4):
                etr = PE.op("transpose", pt[:, bb * 128:(bb + 1) * 128],
                            s_act[aslot][:, bb * 128:(bb + 1) * 128],
                            s_ident[:], waits=wtr if bb == 0 else [],
                            inc=(bb == 3))
            e_stg2_free[aslot] = etr
            wst = [etr]
            if e_stg_free[tsl]:
                wst.append(e_stg_free[tsl])
            e_cp = V.op("tensor_copy", s_stg[tsl][:], pt[:, 0:512],
                        waits=wst)
            e_tr_free[tsl] = e_cp
            dst = AP(dram_t.ap().tensor,
                     dram_t.ap().offset
                     + (th_ * 4) * D + eb_ * 256 + et_ * 128,
                     [[8 * D, 128], [D, 4], [1, 128]])
            e_stq = SY.dma(dst,
                           s_stg[tsl][:].rearrange("p (b c) -> p b c", c=128),
                           waits=[e_cp], sem2=stgs_st[tsl])
            e_stg_free[tsl] = e_stq
            st_last[0] = e_stq
            trslot += 1

        for eb in range(16):
            if eb >= 2:
                rb, ev, rslot = ring_load(wo_in, eb)
                wo_ld[eb] = (rb, ev)
            rbuf, e_wo_ld = wo_ld[eb]
            last_use = None
            for et in range(2):
                for th in range(2):
                    ps = ps4[pslot % 4]
                    w = [e_wo_ld, e_atF]
                    if pev4[pslot % 4]:
                        w.append(pev4[pslot % 4])
                    e_mm = None
                    for dt in range(32):
                        e_mm = PE.op(
                            "matmul", ps,
                            rbuf[:, dt * 256 + et * 128:
                                 dt * 256 + et * 128 + 128],
                            s_attnF[:, dt * T + th * 512:dt * T + th * 512 + 512],
                            start=(dt == 0), stop=(dt == 31),
                            waits=w if dt == 0 else [], inc=(dt == 31))
                    last_use = e_mm
                    flush4(attn_tok_d)
                    aslot = pslot % 2
                    w2 = [e_mm]
                    if e_stg2_free[aslot]:
                        w2.append(e_stg2_free[aslot])
                    e_ev = SC.op("activation", s_act[aslot][:], ps,
                                 AF.Copy, waits=w2)
                    pev4[pslot % 4] = e_ev
                    pend4 = (aslot, e_ev, th, eb, et)
                    pslot += 1
            ring_free[eb % 2] = last_use
        flush4(attn_tok_d)
        e_wo_st_all = [(stgs_st[0][0], stgs_st[0][1][0]),
                       (stgs_st[1][0], stgs_st[1][1][0])]

        prev_x0 = None
        prev_att = None
        prev_x1_readers = None
        e_x1_sp_last = None
        for tt in range(8):
            w = list(e_wo_st_all)
            if prev_x0:
                w.append(prev_x0)
            e_x0 = SY.dma(s_x0t[:], x_my[tt * 128:(tt + 1) * 128, :],
                          waits=w, sem2=q_x0t)
            w = list(e_wo_st_all)
            if prev_att:
                w.append(prev_att)
            e_a2 = SY.dma(s_att[:],
                          attn_tok_d.ap()[:, tt * D:(tt + 1) * D],
                          waits=w, sem2=q_att)
            w = [e_x0, e_a2]
            if prev_x1_readers:
                w += prev_x1_readers
            e_x1 = V.op("tensor_add", s_x1t[:], s_att[:], s_x0t[:], waits=w)
            prev_x0 = e_x1
            prev_att = e_x1
            e_x1_sp_last = SY.dma(x1_d.ap()[:, tt * D:(tt + 1) * D],
                                  s_x1t[:], waits=[e_x1], sem2=q_sp)
            if DEBUG_TAPS:
                SY.dma(taps["x1"][:, tt * D:(tt + 1) * D], s_x1t[:],
                       waits=[e_x1])
            e_bn = None
            for i in range(8):
                e_bn = V.op("bn_stats", s_bnst[:, i * 6:(i + 1) * 6],
                            s_x1t[:, i * 512:(i + 1) * 512],
                            waits=[e_x1] if i == 0 else [])
            V.op("bn_aggr", s_ln[:, 0:2], s_bnst[:])
            V.op("tensor_scalar_add", s_ln[:, 2:3], s_ln[:, 1:2], EPS)
            e_rc = V.op("reciprocal", s_ln[:, 3:4], s_ln[:, 2:3])
            e_sq = SC.op("activation", s_ln[:, 4:5], s_ln[:, 3:4], AF.Sqrt,
                         waits=[e_rc])
            e_nm = V.op("tensor_scalar_mul", s_ln[:, 5:6], s_ln[:, 0:1], -1.0)
            last_apply = None
            for q in range(8):
                sl = q % 2
                w = [e_nm, e_sq]
                if e_stg_free[sl]:
                    w.append(e_stg_free[sl])
                e_ap = V.op("tensor_scalar", s_stg[sl][:],
                            s_x1t[:, q * 512:(q + 1) * 512],
                            s_ln[:, 5:6], s_ln[:, 4:5], ALU.add, ALU.mult,
                            waits=w)
                last_apply = e_ap
                pt = psCD_bf[sl]
                w = [e_ap]
                if e_tr_free[sl]:
                    w.append(e_tr_free[sl])
                etr = None
                for b in range(4):
                    etr = PE.op("transpose", pt[:, b * 128:(b + 1) * 128],
                                s_stg[sl][:, b * 128:(b + 1) * 128],
                                s_ident[:], waits=w if b == 0 else [],
                                inc=(b == 3))
                e_stg_free[sl] = etr
                _b = s_xt2[:]
                dst = AP(_b.tensor, _b.offset + q * 4 * 1024 + tt * 128,
                         [[32 * T, 128], [1024, 4], [1, 128]])
                e_ev = SC.op("activation", dst,
                             pt[:, 0:512].rearrange("p (b c) -> p b c", c=128),
                             AF.Copy, waits=[etr])
                e_tr_free[sl] = e_ev
            prev_x1_readers = [last_apply, e_x1_sp_last]
        e_xt2_done = [e for e in e_tr_free if e]

        inputs_l = [s_xt2, s_xx1, s_xx2]
        outputs_l = [s_xx1, s_xx2, None]
        e_inp_done = {0: e_xt2_done}
        e_S_ev = [None, None]
        e_last_corr = None
        nblk = 16
        for l in range(3):
            inp = inputs_l[l]
            for th in range(2):
                w = list(e_inp_done[l]) + [e_om]
                if e_denfree[0]:
                    w.append(e_denfree[0])
                e_rs = None
                for dt in range(32):
                    e_rs = PE.op("matmul", psE[0][:], s_ones_mat[:, :],
                                 inp[:, dt * T + th * 512:
                                     dt * T + th * 512 + 512],
                                 start=(dt == 0), stop=(dt == 31),
                                 waits=w if dt == 0 else [], inc=(dt == 31))
                ws = [e_rs]
                if e_last_corr:
                    ws.append(e_last_corr)
                e_S_ev[th] = SC.op("activation",
                                   s_S[:, th * 512:(th + 1) * 512],
                                   psE[0][0:1, :], AF.Copy, waits=ws)
                e_denfree[0] = e_S_ev[th]
            nxt_written = []
            for eb in range(nblk):
                rbuf, e_w, rslot5 = ring_load(km_in[l], eb)
                last_use = None
                for et in range(2):
                    e_t = eb * 2 + et
                    for th in range(2):
                        ps = ps4[pslot % 4]
                        w = [e_w, e_S_ev[th], e_nh]
                        if pev4[pslot % 4]:
                            w.append(pev4[pslot % 4])
                        e_mm = None
                        for dt in range(32):
                            e_mm = PE.op(
                                "matmul", ps,
                                rbuf[:, dt * 256 + et * 128:
                                     dt * 256 + et * 128 + 128],
                                inp[:, dt * T + th * 512:
                                    dt * T + th * 512 + 512],
                                start=(dt == 0), stop=False,
                                waits=w if dt == 0 else [], inc=False)
                        e_mm = PE.op("matmul", ps, s_negh[:, :],
                                     s_S[:, th * 512:(th + 1) * 512],
                                     start=False, stop=True)
                        e_last_corr = e_mm
                        last_use = e_mm
                        if l == 2:
                            flush4(ffn_tok_d)
                        aslot = pslot % 2
                        w2 = [e_mm, e_sv2]
                        if e_stg2_free[aslot]:
                            w2.append(e_stg2_free[aslot])
                        if l < 2:
                            e_ev = SC.op("activation", s_act[aslot][:], ps,
                                         AF.Relu, scale=s_sv2[:, l:l + 1],
                                         waits=w2)
                            pev4[pslot % 4] = e_ev
                            wv = [e_ev, e_cv]
                            if l == 0 and e_x1_sp_last:
                                wv.append(e_x1_sp_last)
                            if l == 0:
                                wv += e_xt2_done
                            e_wr = V.op(
                                "tensor_scalar",
                                outputs_l[l][:, e_t * T + th * 512:
                                             e_t * T + th * 512 + 512],
                                s_act[aslot][:],
                                s_cvec[:, e_t:e_t + 1], None, ALU.mult,
                                waits=wv)
                            e_stg2_free[aslot] = e_wr
                            nxt_written.append(e_wr)
                        else:
                            e_ev = SC.op("activation", s_act[aslot][:], ps,
                                         AF.Copy, waits=w2)
                            pev4[pslot % 4] = e_ev
                            pend4 = (aslot, e_ev, th, eb, et)
                        pslot += 1
                ring_free[rslot5] = last_use
            if l == 2:
                flush4(ffn_tok_d)
                e_ffn_all = [(stgs_st[0][0], stgs_st[0][1][0]),
                             (stgs_st[1][0], stgs_st[1][1][0])]
            if l < 2:
                e_inp_done[l + 1] = nxt_written[-1:]

        prev_yr = None
        prev_out = None
        e_out_dmas = []
        for tt in range(8):
            w = list(e_ffn_all)
            if prev_yr:
                w.append(prev_yr)
            e_f = SY.dma(s_fft[:], ffn_tok_d.ap()[:, tt * D:(tt + 1) * D],
                         waits=w, sem2=q_ff)
            w2 = [e_x1_sp_last] + e_ffn_all
            if prev_yr:
                w2.append(prev_yr)
            e_x1l = SY.dma(s_x1t[:], x1_d.ap()[:, tt * D:(tt + 1) * D],
                           waits=w2, sem2=q_xl)
            w3 = [e_f, e_x1l, e_sv2]
            if prev_out:
                w3.append(prev_out)
            e_y = V.op("scalar_tensor_tensor", s_yt[:], s_fft[:],
                       s_sv2[:, 2:3], s_x1t[:], ALU.mult, ALU.add, waits=w3)
            prev_yr = e_y
            e_o = SY.dma(y_my[tt * 128:(tt + 1) * 128, :], s_yt[:],
                         waits=[e_y])
            prev_out = e_o
            e_out_dmas.append(e_o)

        SY.wait([e_out_dmas[-1]])
        G.wait([e_out_dmas[-1]])
        return

    block_ctx = contextlib.ExitStack()
    block = block_ctx.enter_context(nc.Block())

    @block.sync
    def _(sync):
        build_all({"sync": sync})

    @block.gpsimd
    def _(gpsimd):
        build_all({"gpsimd": gpsimd})

    @block.vector
    def _(vector):
        build_all({"vector": vector})

    @block.scalar
    def _(scalar):
        build_all({"scalar": scalar})

    @block.tensor
    def _(tensor):
        build_all({"tensor": tensor})

    block_ctx.close()
    sctx.close()
    pctx.close()
    return nc


_NC_CACHE = None


def kernel(**inputs):
    global _NC_CACHE
    x = np.ascontiguousarray(
        np.asarray(inputs["L0_attn_hidden_states"], np.float32).reshape(S, D))
    wq = np.asarray(inputs["L0_attn_wq"], np.float32)
    wkv = np.asarray(inputs["L0_attn_wkv"], np.float32)
    wo = np.ascontiguousarray(np.asarray(inputs["L0_attn_wo"], np.float32))
    km = [np.ascontiguousarray(
        np.asarray(inputs[f"L0_ffn_k{i}"]).astype(np.uint8)) for i in range(3)]
    s3 = np.array([[float(np.asarray(inputs["L0_ffn_s0"])),
                    float(np.asarray(inputs["L0_ffn_s1"])),
                    float(np.asarray(inputs["L0_ffn_s2"]))]], np.float32)

    wq_r = wq.reshape(D, 32, 192)[:, :, :128]
    wk_r = wkv[:, :4096].reshape(D, 32, 128)
    wv_r = wkv[:, 4096:8192].reshape(D, 32, 128)

    def _tile_pm(w):
        return np.ascontiguousarray(
            w.reshape(32, 128, 512).transpose(1, 0, 2).reshape(128, 32 * 512))

    def _tile_blocks(w):
        return np.ascontiguousarray(
            w.reshape(32, 128, 16, 256).transpose(2, 1, 0, 3).reshape(
                2048, 32 * 256))

    wo_t = _tile_blocks(wo)
    km_t = [_tile_blocks(k) for k in km]
    wq_tt = _tile_blocks(np.ascontiguousarray(wq_r.reshape(D, D)))
    wk_tt = _tile_blocks(np.ascontiguousarray(wk_r.reshape(D, D)))
    wv_tt = _tile_blocks(np.ascontiguousarray(wv_r.reshape(D, D)))

    if _NC_CACHE is None:
        _NC_CACHE = build_nc()
    nc = _NC_CACHE

    in_maps = []
    for c in range(NCORES):
        xm = np.concatenate([x[c * CH:(c + 1) * CH],
                             x[(15 - c) * CH:(16 - c) * CH]], 0)
        in_maps.append({
            "x_my": np.ascontiguousarray(xm),
            "wq_t": wq_tt, "wk_t": wk_tt, "wv_t": wv_tt,
            "wo_in": wo_t,
            "k0_in": km_t[0], "k1_in": km_t[1], "k2_in": km_t[2],
            "scal3": s3,
        })

    res = run_bass_kernel_spmd(nc, in_maps, core_ids=list(range(NCORES)))

    y = np.empty((S, D), np.float32)
    for c in range(NCORES):
        ym = np.asarray(res.results[c]["y_my"])
        y[c * CH:(c + 1) * CH] = ym[:CH]
        y[(15 - c) * CH:(16 - c) * CH] = ym[CH:]
    if DEBUG_TAPS:
        kernel.last_results = res
    return y.reshape(1, S, D)


# version 32 (final)
# speedup vs baseline: 1.0248x; 1.0248x over previous
import numpy as np
import ml_dtypes
import concourse.bass as bass
import concourse.mybir as mybir
from concourse.bass_utils import run_bass_kernel_spmd
from concourse.aot_env import allow_ici_all_to_all

F32 = mybir.dt.float32
BF16 = mybir.dt.bfloat16
U8 = mybir.dt.uint8
AF = mybir.ActivationFunctionType
ALU = mybir.AluOpType
AP = bass.AP

NCORES = 8
H = 32
HPC = 4
D = 4096
S = 8192
T = 1024
CH = 512
NCHUNK = 16
QKSCALE = float(192.0 ** -0.5)
EXPBIAS = -12.0
NE = 3277
EPS = 1e-5

DEBUG_TAPS = False


def ppos(g):
    return 2 * g if g <= 7 else 2 * (15 - g) + 1


class CEng:

    def __init__(self, sem, self_chain=True):
        self.sem, self.n, self.e, self.lastw = sem, 0, None, {}
        self.self_chain = self_chain

    def op(self, fname, *args, waits=(), inc=True, **kw):
        if inc:
            self.n += 1
        if self.e is not None:
            if self.self_chain and inc and self.n > 1:
                self.e.wait_ge(self.sem, self.n - 1)
            for ev in waits:
                if ev is None:
                    continue
                sem, cnt = ev
                if sem is self.sem or cnt <= 0:
                    continue
                if cnt > self.lastw.get(sem.name, 0):
                    self.e.wait_ge(sem, cnt)
                    self.lastw[sem.name] = cnt
            ins = getattr(self.e, fname)(*args, **kw)
            if inc:
                ins.then_inc(self.sem, 1)
        return (self.sem, self.n)

    @property
    def now(self):
        return (self.sem, self.n)


class QEng:

    def __init__(self, sem):
        self.sem, self.n, self.e, self.lastw = sem, 0, None, {}

    def _waits(self, waits):
        for ev in waits:
            if ev is None:
                continue
            sem, cnt = ev
            if cnt <= 0:
                continue
            if cnt > self.lastw.get(sem.name, 0):
                self.e.wait_ge(sem, cnt)
                self.lastw[sem.name] = cnt

    def dma(self, out, in_, waits=(), sem2=None, **kw):
        if sem2 is not None:
            sem, cnt = sem2
            cnt[0] += 16
            if self.e is not None:
                self._waits(waits)
                self.e.dma_start(out=out, in_=in_, **kw).then_inc(sem, 16)
            return (sem, cnt[0])
        self.n += 16
        if self.e is not None:
            self._waits(waits)
            self.e.dma_start(out=out, in_=in_, **kw).then_inc(self.sem, 16)
        return (self.sem, self.n)

    def cc(self, kind, rg, ins, outs, ccsem, waits=()):
        if self.e is not None:
            self._waits(waits)
            with allow_ici_all_to_all():
                self.e.collective_compute(
                    kind, ALU.bypass, replica_groups=rg, ins=ins, outs=outs,
                ).then_inc(ccsem, 1)
        return (ccsem, 1)

    def wait(self, waits):
        if self.e is not None:
            self._waits(waits)

    @property
    def now(self):
        return (self.sem, self.n)


def _host_consts():
    ident = np.eye(128, dtype=ml_dtypes.bfloat16)
    masks = np.zeros((128, 4 * CH), dtype=ml_dtypes.bfloat16)
    kk = np.arange(128)[:, None]
    qq = np.arange(CH)[None, :]
    for o in range(4):
        masks[:, o * CH:(o + 1) * CH] = ((o * 128 + kk) <= qq).astype(
            ml_dtypes.bfloat16)
    cvec = np.empty((128, 32), dtype=np.float32)
    for t in range(32):
        d = t * 128 + np.arange(128)
        cvec[:, t] = np.where(d < NE, 2.0, -8.0)
    return ident, masks, cvec


def build_nc():
    nc = bass.Bass("TRN2", target_bir_lowering=False, debug=False)

    x_my = nc.dram_tensor("x_my", [T, D], F32, kind="ExternalInput").ap()
    wq_t = nc.dram_tensor("wq_t", [16 * 128, 32 * 256], F32,
                          kind="ExternalInput").ap()
    wk_t = nc.dram_tensor("wk_t", [16 * 128, 32 * 256], F32,
                          kind="ExternalInput").ap()
    wv_t = nc.dram_tensor("wv_t", [16 * 128, 32 * 256], F32,
                          kind="ExternalInput").ap()
    wo_in = nc.dram_tensor("wo_in", [16 * 128, 32 * 256], F32,
                           kind="ExternalInput").ap()
    km_in = [nc.dram_tensor(f"k{i}_in", [16 * 128, 32 * 256], U8,
                            kind="ExternalInput").ap() for i in range(3)]
    scal3 = nc.dram_tensor("scal3", [1, 3], F32, kind="ExternalInput").ap()
    y_my = nc.dram_tensor("y_my", [T, D], F32, kind="ExternalOutput").ap()

    cc_q_in = nc.dram_tensor("cc_q_in", [NCORES * 128, HPC * T], BF16)
    cc_q_out = nc.dram_tensor("cc_q_out", [NCORES * 128, HPC * T], BF16)
    cc_k_in = nc.dram_tensor("cc_k_in", [NCORES * 128, HPC * T], BF16)
    cc_k_out = nc.dram_tensor("cc_k_out", [NCORES * 128, HPC * T], BF16)
    cc_vin2 = nc.dram_tensor("cc_vin2", [NCORES * 128, HPC * T], BF16)
    cc_vout2 = nc.dram_tensor("cc_vout2", [NCORES * 128, HPC * T], BF16)
    cc_at_in = [nc.dram_tensor(f"cc_at_in{i}", [NCORES * 128, 2 * T], BF16)
                for i in range(2)]
    cc_at_out = [nc.dram_tensor(f"cc_at_out{i}", [NCORES * 128, 2 * T], BF16)
                 for i in range(2)]
    attn_tok_d = nc.dram_tensor("attn_tok_d", [128, 8 * D], BF16)
    x1_d = nc.dram_tensor("x1_d", [128, 8 * D], F32)
    ffn_tok_d = nc.dram_tensor("ffn_tok_d", [128, 8 * D], BF16)

    ident_np, masks_np, cvec_np = _host_consts()
    ident_dr = nc.inline_tensor(ident_np, "ident")
    masks_dr = nc.inline_tensor(masks_np, "masks")
    cvec_dr = nc.inline_tensor(cvec_np, "cvec")

    taps = {}
    if DEBUG_TAPS:
        taps["agout"] = nc.dram_tensor("tap_agout", [NCORES * 128, 32 * T],
                                       BF16, kind="ExternalOutput").ap()
        taps["qt"] = nc.dram_tensor("tap_qt", [128, HPC * S], BF16,
                                    kind="ExternalOutput").ap()
        taps["kt"] = nc.dram_tensor("tap_kt", [128, HPC * S], BF16,
                                    kind="ExternalOutput").ap()
        taps["v"] = nc.dram_tensor("tap_v", [128, HPC * S], BF16,
                                   kind="ExternalOutput").ap()
        taps["attnT"] = nc.dram_tensor("tap_attnT", [128, HPC * S], BF16,
                                       kind="ExternalOutput").ap()
        taps["atout"] = nc.dram_tensor("tap_atout", [NCORES * 128, HPC * T],
                                       BF16, kind="ExternalOutput").ap()
        taps["x1"] = nc.dram_tensor("tap_x1", [128, 8 * D], F32,
                                    kind="ExternalOutput").ap()

    BASE = 16512
    bump = [BASE]

    def at(name, shape, dtype, region_off=None):
        o = bump[0] if region_off is None else region_off
        t = nc.alloc_sbuf_tensor_at(name, shape, dtype, offset=o)
        if region_off is None:
            nbytes = int(np.prod(shape[1:])) * mybir.dt.size(dtype)
            bump[0] = (o + nbytes + 31) & ~31
        return t

    s_ident = at("s_ident", [128, 128], BF16)
    s_masks = at("s_masks", [128, 4 * CH], BF16)
    s_cvec = at("s_cvec", [128, 32], F32)
    s_expb = at("s_expb", [128, 1], F32)
    s_ones_c_f32 = at("s_ones_c_f32", [128, 1], F32)
    s_ones_c_bf = at("s_ones_c_bf", [128, 1], BF16)
    s_ones_r_f32 = at("s_ones_r_f32", [1, 128], F32)
    s_ones_mat = at("s_ones_mat", [128, 128], BF16)
    s_negh = at("s_negh", [1, 128], F32)
    s_sc3 = at("s_sc3", [1, 3], F32)
    s_sv = at("s_sv", [128, 4], F32)
    s_sv2 = at("s_sv2", [128, 4], F32)
    s_ptp = [at(f"s_ptp{i}", [128, 2 * CH], BF16) for i in range(4)]
    s_ptm = [at(f"s_ptm{i}", [128, 2 * CH], BF16) for i in range(2)]
    s_recb = [at(f"s_recb{i}", [128, CH], F32) for i in range(2)]
    s_S = at("s_S", [1, 2 * CH], F32)
    s_act = [at(f"s_act{i}", [128, CH], BF16) for i in range(2)]
    s_stg = [at(f"s_stg{i}", [128, CH], BF16) for i in range(2)]
    s_ln = at("s_ln", [128, 8], F32)
    s_bnst = at("s_bnst", [128, 48], F32)
    A0 = (bump[0] + 31) & ~31
    B0 = A0 + 64 * 1024
    C0 = B0 + 64 * 1024
    E0 = C0 + 32 * 1024
    assert E0 + 16 * 1024 <= 229344, (A0, E0)

    s_xt = at("s_xt", [128, 32 * T], BF16, A0)
    s_agh = [at(f"s_agh{i}", [128, 32 * 512], BF16, A0 + i * 32 * 1024)
             for i in range(2)]
    s_kh = at("s_kh", [128, S], BF16, A0)
    s_vh = at("s_vh", [128, S], BF16, A0 + 16 * 1024)
    s_qh = at("s_qh", [128, S], BF16, A0 + 32 * 1024)
    s_xt2 = at("s_xt2", [128, 32 * T], BF16, A0)
    s_xx2 = at("s_xx2", [128, 32 * T], BF16, A0)

    s_wq = at("s_wq", [128, 32 * 512], BF16, B0)
    s_wk = at("s_wk", [128, 32 * 512], BF16, B0 + 32 * 1024)
    s_attnT = at("s_attnT", [128, HPC * S], BF16, B0)
    s_attnF = at("s_attnF", [128, 32 * T], BF16, B0)
    s_xx1 = at("s_xx1", [128, 32 * T], BF16, B0)
    s_x0t = at("s_x0t", [128, D], F32, B0)
    s_att = at("s_att", [128, D], BF16, B0 + 16 * 1024)
    s_x1t = at("s_x1t", [128, D], F32, B0 + 24 * 1024)
    s_x1h = at("s_x1h", [128, 4 * D], F32, B0)

    s_wv = at("s_wv", [128, 32 * 512], BF16, C0)
    s_wring = [at(f"s_wring{i}", [128, 32 * 256], BF16, C0 + i * 16 * 1024)
               for i in range(2)]

    s_x0 = at("s_x0", [128, D], F32, E0)
    s_dnb = [at(f"s_dnb{i}", [128, CH], F32, E0 + i * 2048) for i in range(2)]
    s_stgf = [at(f"s_stgf{i}", [128, CH], F32, E0 + 4096 + i * 2048)
              for i in range(2)]

    import contextlib
    pctx = contextlib.ExitStack()
    psQ = [pctx.enter_context(nc.psum_tensor(f"psQ{i}", [128, 1024], F32))
           for i in range(2)]
    psC = pctx.enter_context(nc.psum_tensor("psC", [128, 512], F32))
    psD = pctx.enter_context(nc.psum_tensor("psD", [128, 512], F32))
    psE = [pctx.enter_context(nc.psum_tensor(f"psE{i}", [128, 512], F32))
           for i in range(2)]
    ps4 = [psQ[0][:, 0:512], psQ[0][:, 512:1024],
           psQ[1][:, 0:512], psQ[1][:, 512:1024]]
    psCD_bf = [psC[:, :].bitcast(BF16), psD[:, :].bitcast(BF16)]

    sctx = contextlib.ExitStack()
    sem = {k: sctx.enter_context(nc.semaphore(k))
           for k in ["pe", "ve", "se", "sy", "gp", "ccq", "cck", "ccv",
                     "ccat", "ccat2", "stgq0", "stgq1", "stgk0", "stgk1",
                     "stgv0", "stgv1", "stgs0", "stgs1", "stg",
                     "qx0", "qat", "qsp", "qxh", "qout"]}
    rg = [list(range(NCORES))]

    def build_all(live):
        PE = CEng(sem["pe"], self_chain=False)
        V, SC = CEng(sem["ve"]), CEng(sem["se"])
        stg2 = (sem["stg"], [0])
        q_x0t = (sem["qx0"], [0])
        q_att = (sem["qat"], [0])
        q_sp = (sem["qsp"], [0])
        q_xh = (sem["qxh"], [0])
        q_out = (sem["qout"], [0])
        SY, G = QEng(sem["sy"]), QEng(sem["gp"])
        PE.e = live.get("tensor")
        V.e = live.get("vector")
        SC.e = live.get("scalar")
        SY.e = live.get("sync")
        G.e = live.get("gpsimd")

        e_id = SY.dma(s_ident[:], ident_dr.ap())
        e_mk = SY.dma(s_masks[:], masks_dr.ap())
        e_cv = SY.dma(s_cvec[:], cvec_dr.ap())
        e_s3 = SY.dma(s_sc3[:], scal3)
        e_exb = V.op("memset", s_expb[:], EXPBIAS)
        e_o1 = V.op("memset", s_ones_c_f32[:], 1.0)
        e_o2 = V.op("memset", s_ones_c_bf[:], 1.0)
        e_o3 = V.op("memset", s_ones_r_f32[:], 1.0)
        e_nh = V.op("memset", s_negh[:], -0.5)
        e_om = V.op("memset", s_ones_mat[:], 1.0)
        e_bs = PE.op("matmul", psE[0][:, 0:3], s_ones_r_f32[:, :],
                     s_sc3[:, :], start=True, stop=True, waits=[e_o3, e_s3])
        e_sv = SC.op("activation", s_sv[:, 0:3], psE[0][:, 0:3], AF.Copy,
                     waits=[e_bs])
        e_sv2 = V.op("tensor_scalar_mul", s_sv2[:, 0:3], s_sv[:, 0:3], 2.0,
                     waits=[e_sv])


        e_stg_free = [None, None]
        e_tr_free = [None, None]
        prev_x0_reader = None
        for tt in range(8):
            w = [prev_x0_reader] if prev_x0_reader else []
            e_x0 = SY.dma(s_x0[:], x_my[tt * 128:(tt + 1) * 128, :], waits=w)
            e_bn = None
            for i in range(8):
                e_bn = V.op("bn_stats", s_bnst[:, i * 6:(i + 1) * 6],
                            s_x0[:, i * 512:(i + 1) * 512],
                            waits=[e_x0] if i == 0 else [])
            e_ag2 = V.op("bn_aggr", s_ln[:, 0:2], s_bnst[:])
            e_vp = V.op("tensor_scalar_add", s_ln[:, 2:3], s_ln[:, 1:2], EPS)
            e_rc = V.op("reciprocal", s_ln[:, 3:4], s_ln[:, 2:3])
            e_sq = SC.op("activation", s_ln[:, 4:5], s_ln[:, 3:4], AF.Sqrt,
                         waits=[e_rc])
            e_nm = V.op("tensor_scalar_mul", s_ln[:, 5:6], s_ln[:, 0:1], -1.0)
            last_apply = None
            for q in range(8):
                sl = q % 2
                w = [e_nm, e_sq]
                if e_stg_free[sl]:
                    w.append(e_stg_free[sl])
                e_ap = V.op("tensor_scalar", s_stg[sl][:],
                            s_x0[:, q * 512:(q + 1) * 512],
                            s_ln[:, 5:6], s_ln[:, 4:5], ALU.add, ALU.mult,
                            waits=w)
                last_apply = e_ap
                pt = psCD_bf[sl]
                w = [e_ap, e_id]
                if e_tr_free[sl]:
                    w.append(e_tr_free[sl])
                etr = None
                for b in range(4):
                    etr = PE.op("transpose", pt[:, b * 128:(b + 1) * 128],
                                s_stg[sl][:, b * 128:(b + 1) * 128],
                                s_ident[:], waits=w if b == 0 else [],
                                inc=(b == 3))
                e_stg_free[sl] = etr
                _b = s_xt[:]
                dst = AP(_b.tensor, _b.offset + q * 4 * 1024 + tt * 128,
                         [[32 * T, 128], [1024, 4], [1, 128]])
                e_ev = SC.op("activation", dst,
                             pt[:, 0:512].rearrange("p (b c) -> p b c", c=128),
                             AF.Copy, waits=[etr])
                e_tr_free[sl] = e_ev
            prev_x0_reader = last_apply
            if tt == 3:
                e_half0 = [e for e in e_tr_free if e]
        e_half1 = [e for e in e_tr_free if e]

        pev = [e_sv, None, None, None]
        e_stg2_free = [None, None]
        ring_free = [None, None]
        ring_ld = {}
        rblk = 0
        pslot = 0
        e_cc = {}
        e_last_mm = None

        def ring_load(wsrc, eb, waits_extra=()):
            nonlocal rblk
            rb = s_wring[rblk % 2]
            w = list(waits_extra)
            if ring_free[rblk % 2]:
                w.append(ring_free[rblk % 2])
            ev = G.dma(rb[:], wsrc[eb * 128:(eb + 1) * 128, :], waits=w)
            ring_ld[rblk] = ev
            rblk += 1
            return rb, ev, (rblk - 1) % 2

        stgs = {("q", 0): (sem["stgq0"], [0]), ("q", 1): (sem["stgq1"], [0]),
                ("k", 0): (sem["stgk0"], [0]), ("k", 1): (sem["stgk1"], [0]),
                ("v", 0): (sem["stgv0"], [0]), ("v", 1): (sem["stgv1"], [0])}
        for mat, wsrc, ccin, ccsem, ccout in (
                ("q", wq_t, cc_q_in, "ccq", cc_q_out),
                ("k", wk_t, cc_k_in, "cck", cc_k_out),
                ("v", wv_t, cc_vin2, "ccv", cc_vout2)):
            e_stg_last = None
            if mat == "q":
                loop = [(th, eb) for th in range(2) for eb in range(16)]
            else:
                loop = [(th, eb) for eb in range(16) for th in range(2)]
            last_eb = None
            for th, eb in loop:
                if eb != last_eb or (mat == "q"):
                    rb, e_w, rslot = ring_load(wsrc, eb)
                    last_eb = eb
                last_use = None
                for et in range(2):
                    ps = ps4[pslot % 4]
                    wts = [e_w]
                    wts += e_half0 if th == 0 else e_half1
                    if pev[pslot % 4]:
                        wts.append(pev[pslot % 4])
                    e_mm = None
                    for dt in range(32):
                        if mat != "v":
                            e_mm = PE.op(
                                "matmul", ps,
                                rb[:, dt * 256 + et * 128:
                                   dt * 256 + et * 128 + 128],
                                s_xt[:, dt * T + th * 512:
                                     dt * T + th * 512 + 512],
                                start=(dt == 0), stop=(dt == 31),
                                waits=wts if dt == 0 else [],
                                inc=(dt == 31))
                        else:
                            pass
                    if mat == "v":
                        break
                    last_use = e_mm
                    aslot = pslot % 2
                    w2 = [e_mm]
                    if e_stg2_free[aslot]:
                        w2.append(e_stg2_free[aslot])
                    e_ev = SC.op("activation", s_act[aslot][:], ps, AF.Copy,
                                 waits=w2)
                    pev[pslot % 4] = e_ev
                    h = 2 * eb + et
                    d, hlh = h // 4, h % 4
                    _bi = ccin.ap()
                    dst = AP(_bi.tensor,
                             _bi.offset + d * 128 * (HPC * T)
                             + hlh * T + th * 512,
                             [[HPC * T, 128], [1, 512]])
                    e_st = SY.dma(dst, s_act[aslot][:], waits=[e_ev],
                                  sem2=stgs[(mat, aslot)])
                    e_stg2_free[aslot] = e_st
                    e_stg_last = e_st
                    pslot += 1
                if mat == "v":
                    for sub in range(4):
                        tt4 = th * 4 + sub
                        ps = ps4[pslot % 4]
                        wts = [e_w]
                        wts += e_half0 if th == 0 else e_half1
                        if pev[pslot % 4]:
                            wts.append(pev[pslot % 4])
                        e_mm = None
                        for dt in range(32):
                            e_mm = PE.op(
                                "matmul", ps[:, 0:256],
                                s_xt[:, dt * T + tt4 * 128:
                                     dt * T + tt4 * 128 + 128],
                                rb[:, dt * 256:(dt + 1) * 256],
                                start=(dt == 0), stop=(dt == 31),
                                waits=wts if dt == 0 else [],
                                inc=(dt == 31))
                        last_use = e_mm
                        aslot = pslot % 2
                        w2 = [e_mm]
                        if e_stg2_free[aslot]:
                            w2.append(e_stg2_free[aslot])
                        e_ev = SC.op("activation", s_act[aslot][:, 0:256],
                                     ps[:, 0:256], AF.Copy, waits=w2)
                        pev[pslot % 4] = e_ev
                        d, b2 = eb // 2, eb % 2
                        _bi = ccin.ap()
                        dst = AP(_bi.tensor,
                                 _bi.offset + d * 128 * (HPC * T)
                                 + tt4 * 512 + b2 * 256,
                                 [[HPC * T, 128], [1, 256]])
                        e_st = SY.dma(dst, s_act[aslot][:, 0:256],
                                      waits=[e_ev], sem2=stgs[(mat, aslot)])
                        e_stg2_free[aslot] = e_st
                        e_stg_last = e_st
                        pslot += 1
                ring_free[rslot] = last_use
                e_last_mm = last_use
            e_cc[mat] = G.cc(
                "AllToAll", rg, [ccin.ap().opt()], [ccout.ap().opt()],
                sem[ccsem],
                waits=[(stgs[(mat, 0)][0], stgs[(mat, 0)][1][0]),
                       (stgs[(mat, 1)][0], stgs[(mat, 1)][1][0])])
        e_qkv_cc = [e_cc["q"], e_cc["k"], e_cc["v"]]
        e_v_last_mm = e_last_mm

        wo_ld = {}
        for eb in range(2):
            rb, ev, rslot = ring_load(wo_in, eb)
            wo_ld[eb] = (rb, ev)
            ring_free[rslot] = None

        pr_glob = 0
        e_exp_by_par = [None, None]
        e_ptp_free = [[], [], [], []]
        e_ptm_free = [[], []]
        e_attnevac = [e_tr_free[0], e_tr_free[1]]
        e_recb_free = [None, None]
        e_dnb_free = [None, None]
        e_denfree = [e_sv, None]
        e_atst = [None, None]
        e_at = [None, None]
        e_head_last_use = None
        e_head_dma = None
        for hl in range(HPC):
            w = list(e_qkv_cc) + [e_v_last_mm]
            if e_head_last_use:
                w.append(e_head_last_use)
            _k = cc_k_out.ap()
            SY.dma(s_kh[:].rearrange("p (r t) -> p r t", t=T),
                   AP(_k.tensor, _k.offset + hl * T,
                      [[HPC * T, 128], [128 * HPC * T, 8], [1, T]]),
                   waits=w)
            _v = cc_vout2.ap()
            for r in range(8):
                SY.dma(s_vh[:, r * T:(r + 1) * T].rearrange(
                           "p (tt c) -> p tt c", c=128),
                       AP(_v.tensor, _v.offset + r * 128 * (HPC * T)
                          + hl * 128,
                          [[HPC * T, 128], [512, 8], [1, 128]]))
            _q = cc_q_out.ap()
            e_head_dma = SY.dma(
                s_qh[:].rearrange("p (r t) -> p r t", t=T),
                AP(_q.tensor, _q.offset + hl * T,
                   [[HPC * T, 128], [128 * HPC * T, 8], [1, T]]))
            if hl == 3:
                e_at[0] = G.cc("AllToAll", rg, [cc_at_in[0].ap().opt()],
                               [cc_at_out[0].ap().opt()], sem["ccat"],
                               waits=[e_atst[0], e_head_dma])
            head_last = None
            pend = None
            e_pv_done_by_jp = [None, None]

            def flush_pend():
                nonlocal pend, head_last
                if pend is None:
                    return
                (pfin, e_fin, kcols, psAT_, jp_, m0, kmax_,
                 ptsl_, diag_, ptm_) = pend
                pend = None
                e_last = None
                for half in range(2):
                    m = m0 + half
                    w = [e_fin]
                    if m == 0 and e_attnevac[jp_]:
                        w.append(e_attnevac[jp_])
                    PE.op("matmul", psAT_[:],
                          s_vh[:, kcols[half]:kcols[half] + 128],
                          pfin[:, half * 512:(half + 1) * 512],
                          start=(m == 0), stop=(m == kmax_ - 1),
                          waits=w if half == 0 else [], inc=False)
                    wd = []
                    if m == 0:
                        wd.append(e_om)
                        if e_denfree[jp_]:
                            wd.append(e_denfree[jp_])
                    e_last = PE.op("matmul", psE[jp_][:],
                                   s_ones_mat[:, :],
                                   pfin[:, half * 512:(half + 1) * 512],
                                   start=(m == 0), stop=(m == kmax_ - 1),
                                   waits=wd, inc=(half == 1))
                head_last = e_last
                if diag_:
                    e_ptm_free[ptm_] = [e_last]
                else:
                    e_ptp_free[ptsl_] = [e_last]
                e_pv_done_by_jp[jp_] = e_last
            for j in range(NCHUNK):
                jp = j % 2
                qcol = ppos(j) * CH
                psAT = psC if jp == 0 else psD
                kmax = 4 * (j + 1)
                for pr in range(kmax // 2):
                    par = pr_glob % 2
                    psq = psQ[par]
                    ptsl = pr_glob % 4
                    e_sc = None
                    for half in range(2):
                        m = 2 * pr + half
                        g, o = m // 4, m % 4
                        kcol = ppos(g) * CH + o * 128
                        w = [e_head_dma]
                        if e_exp_by_par[par]:
                            w.append(e_exp_by_par[par])
                        e_sc = PE.op("matmul",
                                     psq[:, half * 512:(half + 1) * 512],
                                     s_kh[:, kcol:kcol + 128],
                                     s_qh[:, qcol:qcol + CH],
                                     start=True, stop=True,
                                     waits=w if half == 0 else [],
                                     inc=(half == 1))
                    flush_pend()
                    w = [e_sc, e_exb] + e_ptp_free[ptsl]
                    e_exp = SC.op("activation", s_ptp[ptsl][:], psq[:, :],
                                  AF.Exp, bias=s_expb[:, :], scale=QKSCALE,
                                  waits=w)
                    e_exp_by_par[par] = e_exp
                    diag = ((2 * pr) // 4 == j)
                    pfin, e_fin = s_ptp[ptsl], e_exp
                    if diag:
                        o0 = (2 * pr) % 4
                        e_fin = V.op("tensor_mul", s_ptm[pr % 2][:],
                                     s_ptp[ptsl][:],
                                     s_masks[:, o0 * CH:(o0 + 2) * CH],
                                     waits=[e_exp, e_mk] + e_ptm_free[pr % 2])
                        pfin = s_ptm[pr % 2]
                    kcols = (ppos((2 * pr) // 4) * CH + ((2 * pr) % 4) * 128,
                             ppos((2 * pr + 1) // 4) * CH
                             + ((2 * pr + 1) % 4) * 128)
                    if diag:
                        e_ptp_free[ptsl] = [e_fin]
                    pend = (pfin, e_fin, kcols, psAT, jp, 2 * pr, kmax,
                            ptsl, diag, pr % 2)
                    pr_glob += 1
                flush_pend()
                e_pv_done = e_pv_done_by_jp[jp]
                w = [e_pv_done]
                if e_dnb_free[jp]:
                    w.append(e_dnb_free[jp])
                e_dev = SC.op("activation", s_dnb[jp][:], psE[jp][:],
                              AF.Copy, waits=w)
                e_denfree[jp] = e_dev
                w = [e_dev]
                if e_recb_free[jp]:
                    w.append(e_recb_free[jp])
                e_rcp = V.op("reciprocal", s_recb[jp][:], s_dnb[jp][:],
                             waits=w)
                e_dnb_free[jp] = e_rcp
                e_aev = V.op("tensor_mul",
                             s_attnT[:, hl * S + qcol:hl * S + qcol + CH],
                             psAT[:], s_recb[jp][:],
                             waits=[e_pv_done, e_rcp])
                e_attnevac[jp] = e_aev
                e_recb_free[jp] = e_aev
            if hl == 1 or hl == 3:
                part = hl // 2
                w = [e for e in e_attnevac if e]
                _bi = cc_at_in[part].ap()
                srcd = s_attnT[:].rearrange("p (hl t8) -> p hl t8", t8=S)
                for d in range(8):
                    dst = AP(_bi.tensor, _bi.offset + d * 128 * (2 * T),
                             [[2 * T, 128], [T, 2], [1, T]])
                    e_atst[part] = SY.dma(
                        dst,
                        srcd[:, 2 * part:2 * part + 2, d * T:(d + 1) * T],
                        waits=w if d == 0 else [], sem2=stg2)
                if part == 1:
                    e_at[1] = G.cc("AllToAll", rg, [cc_at_in[1].ap().opt()],
                                   [cc_at_out[1].ap().opt()], sem["ccat2"],
                                   waits=[e_atst[1]])
            e_head_last_use = head_last

        if DEBUG_TAPS:
            SY.dma(taps["attnT"], s_attnT[:],
                   waits=[e for e in e_attnevac if e])
            for part in range(2):
                SY.dma(taps["atout"][:, part * 2 * T:(part + 1) * 2 * T],
                       cc_at_out[part].ap(), waits=[e_at[part]])
        e_atF = None
        for part in range(2):
            _bo = cc_at_out[part].ap()
            _bf = s_attnF[:]
            for r in range(8):
                srcr = AP(_bo.tensor, _bo.offset + r * 128 * (2 * T),
                          [[2 * T, 128], [T, 2], [1, T]])
                dstr = AP(_bf.tensor, _bf.offset + (4 * r + 2 * part) * T,
                          [[32 * T, 128], [T, 2], [1, T]])
                e_atF = SY.dma(dstr, srcr,
                               waits=[e_at[0], e_at[1], e_atst[1]]
                               if r == 0 else [])

        pev4 = [e_exp_by_par[0], e_exp_by_par[0],
                e_exp_by_par[1], e_exp_by_par[1]]
        e_tr_free = [e_attnevac[0], e_attnevac[1]]
        pslot = 0
        trslot = 0
        pend4 = None
        st_last = [None]
        stgs_st = [(sem["stgs0"], [0]), (sem["stgs1"], [0])]

        def flush4(dram_t, fuse_out=None):
            nonlocal pend4, trslot
            if pend4 is None:
                return
            (aslot, e_ev, th_, eb_, et_) = pend4
            pend4 = None
            tsl = trslot % 2
            pt = psCD_bf[tsl]
            wtr = [e_ev]
            if e_tr_free[tsl]:
                wtr.append(e_tr_free[tsl])
            etr = None
            for bb in range(4):
                etr = PE.op("transpose", pt[:, bb * 128:(bb + 1) * 128],
                            s_act[aslot][:, bb * 128:(bb + 1) * 128],
                            s_ident[:], waits=wtr if bb == 0 else [],
                            inc=(bb == 3))
            e_stg2_free[aslot] = etr
            wst = [etr]
            if e_stg_free[tsl]:
                wst.append(e_stg_free[tsl])
            if fuse_out is None:
                e_cp = V.op("tensor_copy", s_stg[tsl][:], pt[:, 0:512],
                            waits=wst)
                e_tr_free[tsl] = e_cp
                dst = AP(dram_t.ap().tensor,
                         dram_t.ap().offset
                         + (th_ * 4) * D + eb_ * 256 + et_ * 128,
                         [[8 * D, 128], [D, 4], [1, 128]])
                e_stq = SY.dma(
                    dst, s_stg[tsl][:].rearrange("p (b c) -> p b c", c=128),
                    waits=[e_cp], sem2=stgs_st[tsl])
                e_stg_free[tsl] = e_stq
                st_last[0] = e_stq
            else:
                e_x1h_ev = fuse_out
                _bx = s_x1h[:]
                in1 = AP(_bx.tensor, _bx.offset + eb_ * 256 + et_ * 128,
                         [[4 * D, 128], [D, 4], [1, 128]])
                e_cp = V.op("scalar_tensor_tensor", s_stgf[tsl][:]
                            .rearrange("p (b c) -> p b c", c=128),
                            pt[:, 0:512].rearrange("p (b c) -> p b c", c=128),
                            s_sv2[:, 2:3], in1, ALU.mult, ALU.add,
                            waits=wst + [e_x1h_ev, e_sv2])
                e_tr_free[tsl] = e_cp
                dst = AP(y_my.tensor,
                         y_my.offset + (th_ * 512) * D + eb_ * 256
                         + et_ * 128,
                         [[D, 128], [128 * D, 4], [1, 128]])
                e_stq = SY.dma(
                    dst, s_stgf[tsl][:].rearrange("p (b c) -> p b c", c=128),
                    waits=[e_cp], sem2=q_out)
                e_stg_free[tsl] = e_stq
                st_last[0] = e_stq
            trslot += 1

        for eb in range(16):
            if eb >= 2:
                rb, ev, rslot = ring_load(wo_in, eb)
                wo_ld[eb] = (rb, ev)
            rbuf, e_wo_ld = wo_ld[eb]
            last_use = None
            for et in range(2):
                for th in range(2):
                    ps = ps4[pslot % 4]
                    w = [e_wo_ld, e_atF]
                    if pev4[pslot % 4]:
                        w.append(pev4[pslot % 4])
                    e_mm = None
                    for dt in range(32):
                        e_mm = PE.op(
                            "matmul", ps,
                            rbuf[:, dt * 256 + et * 128:
                                 dt * 256 + et * 128 + 128],
                            s_attnF[:, dt * T + th * 512:dt * T + th * 512 + 512],
                            start=(dt == 0), stop=(dt == 31),
                            waits=w if dt == 0 else [], inc=(dt == 31))
                    last_use = e_mm
                    flush4(attn_tok_d)
                    aslot = pslot % 2
                    w2 = [e_mm]
                    if e_stg2_free[aslot]:
                        w2.append(e_stg2_free[aslot])
                    e_ev = SC.op("activation", s_act[aslot][:], ps,
                                 AF.Copy, waits=w2)
                    pev4[pslot % 4] = e_ev
                    pend4 = (aslot, e_ev, th, eb, et)
                    pslot += 1
            ring_free[eb % 2] = last_use
        flush4(attn_tok_d)
        e_wo_st_all = [(stgs_st[0][0], stgs_st[0][1][0]),
                       (stgs_st[1][0], stgs_st[1][1][0])]

        prev_x0 = None
        prev_att = None
        prev_x1_readers = None
        e_x1_sp_last = None
        for tt in range(8):
            w = list(e_wo_st_all)
            if prev_x0:
                w.append(prev_x0)
            e_x0 = SY.dma(s_x0t[:], x_my[tt * 128:(tt + 1) * 128, :],
                          waits=w, sem2=q_x0t)
            w = list(e_wo_st_all)
            if prev_att:
                w.append(prev_att)
            e_a2 = SY.dma(s_att[:],
                          attn_tok_d.ap()[:, tt * D:(tt + 1) * D],
                          waits=w, sem2=q_att)
            w = [e_x0, e_a2]
            if prev_x1_readers:
                w += prev_x1_readers
            e_x1 = V.op("tensor_add", s_x1t[:], s_att[:], s_x0t[:], waits=w)
            prev_x0 = e_x1
            prev_att = e_x1
            e_x1_sp_last = SY.dma(x1_d.ap()[:, tt * D:(tt + 1) * D],
                                  s_x1t[:], waits=[e_x1], sem2=q_sp)
            if DEBUG_TAPS:
                SY.dma(taps["x1"][:, tt * D:(tt + 1) * D], s_x1t[:],
                       waits=[e_x1])
            e_bn = None
            for i in range(8):
                e_bn = V.op("bn_stats", s_bnst[:, i * 6:(i + 1) * 6],
                            s_x1t[:, i * 512:(i + 1) * 512],
                            waits=[e_x1] if i == 0 else [])
            V.op("bn_aggr", s_ln[:, 0:2], s_bnst[:])
            V.op("tensor_scalar_add", s_ln[:, 2:3], s_ln[:, 1:2], EPS)
            e_rc = V.op("reciprocal", s_ln[:, 3:4], s_ln[:, 2:3])
            e_sq = SC.op("activation", s_ln[:, 4:5], s_ln[:, 3:4], AF.Sqrt,
                         waits=[e_rc])
            e_nm = V.op("tensor_scalar_mul", s_ln[:, 5:6], s_ln[:, 0:1], -1.0)
            last_apply = None
            for q in range(8):
                sl = q % 2
                w = [e_nm, e_sq]
                if e_stg_free[sl]:
                    w.append(e_stg_free[sl])
                e_ap = V.op("tensor_scalar", s_stg[sl][:],
                            s_x1t[:, q * 512:(q + 1) * 512],
                            s_ln[:, 5:6], s_ln[:, 4:5], ALU.add, ALU.mult,
                            waits=w)
                last_apply = e_ap
                pt = psCD_bf[sl]
                w = [e_ap]
                if e_tr_free[sl]:
                    w.append(e_tr_free[sl])
                etr = None
                for b in range(4):
                    etr = PE.op("transpose", pt[:, b * 128:(b + 1) * 128],
                                s_stg[sl][:, b * 128:(b + 1) * 128],
                                s_ident[:], waits=w if b == 0 else [],
                                inc=(b == 3))
                e_stg_free[sl] = etr
                _b = s_xt2[:]
                dst = AP(_b.tensor, _b.offset + q * 4 * 1024 + tt * 128,
                         [[32 * T, 128], [1024, 4], [1, 128]])
                e_ev = SC.op("activation", dst,
                             pt[:, 0:512].rearrange("p (b c) -> p b c", c=128),
                             AF.Copy, waits=[etr])
                e_tr_free[sl] = e_ev
            prev_x1_readers = [last_apply, e_x1_sp_last]
        e_xt2_done = [e for e in e_tr_free if e]

        inputs_l = [s_xt2, s_xx1, s_xx2]
        outputs_l = [s_xx1, s_xx2, None]
        e_inp_done = {0: e_xt2_done}
        e_S_ev = [None, None]
        e_last_corr = None
        nblk = 16
        e_l1_done = None
        for l in range(3):
            inp = inputs_l[l]
            for th in range(2):
                w = list(e_inp_done[l]) + [e_om]
                if e_denfree[0]:
                    w.append(e_denfree[0])
                e_rs = None
                for dt in range(32):
                    e_rs = PE.op("matmul", psE[0][:], s_ones_mat[:, :],
                                 inp[:, dt * T + th * 512:
                                     dt * T + th * 512 + 512],
                                 start=(dt == 0), stop=(dt == 31),
                                 waits=w if dt == 0 else [], inc=(dt == 31))
                ws = [e_rs]
                if e_last_corr:
                    ws.append(e_last_corr)
                e_S_ev[th] = SC.op("activation",
                                   s_S[:, th * 512:(th + 1) * 512],
                                   psE[0][0:1, :], AF.Copy, waits=ws)
                e_denfree[0] = e_S_ev[th]
            nxt_written = []
            if l < 2:
                loop5 = [(None, eb) for eb in range(nblk)]
            else:
                loop5 = [(th2, eb) for th2 in range(2) for eb in range(nblk)]
            e_x1h = None
            last_th2 = None
            for th2, eb in loop5:
                if l == 2 and th2 != last_th2:
                    flush4(None, fuse_out=e_x1h)
                    w = [(q_sp[0], q_sp[1][0])]
                    if e_l1_done:
                        w.append(e_l1_done)
                    w += [e for e in e_tr_free if e]
                    for i in range(4):
                        e_x1h = SY.dma(
                            s_x1h[:, i * D:(i + 1) * D],
                            x1_d.ap()[:, (th2 * 4 + i) * D:
                                      (th2 * 4 + i + 1) * D],
                            waits=w if i == 0 else [], sem2=q_xh)
                    last_th2 = th2
                rbuf, e_w, rslot5 = ring_load(km_in[l], eb)
                last_use = None
                for et in range(2):
                    e_t = eb * 2 + et
                    for th in ([th2] if th2 is not None else range(2)):
                        ps = ps4[pslot % 4]
                        w = [e_w, e_S_ev[th], e_nh]
                        if pev4[pslot % 4]:
                            w.append(pev4[pslot % 4])
                        e_mm = None
                        for dt in range(32):
                            e_mm = PE.op(
                                "matmul", ps,
                                rbuf[:, dt * 256 + et * 128:
                                     dt * 256 + et * 128 + 128],
                                inp[:, dt * T + th * 512:
                                    dt * T + th * 512 + 512],
                                start=(dt == 0), stop=False,
                                waits=w if dt == 0 else [], inc=False)
                        e_mm = PE.op("matmul", ps, s_negh[:, :],
                                     s_S[:, th * 512:(th + 1) * 512],
                                     start=False, stop=True)
                        e_last_corr = e_mm
                        last_use = e_mm
                        if l == 2:
                            flush4(None, fuse_out=e_x1h)
                        aslot = pslot % 2
                        w2 = [e_mm, e_sv2]
                        if e_stg2_free[aslot]:
                            w2.append(e_stg2_free[aslot])
                        if l < 2:
                            e_ev = SC.op("activation", s_act[aslot][:], ps,
                                         AF.Relu, scale=s_sv2[:, l:l + 1],
                                         waits=w2)
                            pev4[pslot % 4] = e_ev
                            wv = [e_ev, e_cv]
                            if l == 0 and e_x1_sp_last:
                                wv.append(e_x1_sp_last)
                            if l == 0:
                                wv += e_xt2_done
                            e_wr = V.op(
                                "tensor_scalar",
                                outputs_l[l][:, e_t * T + th * 512:
                                             e_t * T + th * 512 + 512],
                                s_act[aslot][:],
                                s_cvec[:, e_t:e_t + 1], None, ALU.mult,
                                waits=wv)
                            e_stg2_free[aslot] = e_wr
                            nxt_written.append(e_wr)
                        else:
                            e_ev = SC.op("activation", s_act[aslot][:], ps,
                                         AF.Copy, waits=w2)
                            pev4[pslot % 4] = e_ev
                            pend4 = (aslot, e_ev, th, eb, et)
                        pslot += 1
                ring_free[rslot5] = last_use
            if l == 2:
                flush4(None, fuse_out=e_x1h)
            if l == 1:
                e_l1_done = last_use
            if l < 2:
                e_inp_done[l + 1] = nxt_written[-1:]

        SY.wait([(q_out[0], q_out[1][0])])
        G.wait([(q_out[0], q_out[1][0])])
        return

    block_ctx = contextlib.ExitStack()
    block = block_ctx.enter_context(nc.Block())

    @block.sync
    def _(sync):
        build_all({"sync": sync})

    @block.gpsimd
    def _(gpsimd):
        build_all({"gpsimd": gpsimd})

    @block.vector
    def _(vector):
        build_all({"vector": vector})

    @block.scalar
    def _(scalar):
        build_all({"scalar": scalar})

    @block.tensor
    def _(tensor):
        build_all({"tensor": tensor})

    block_ctx.close()
    sctx.close()
    pctx.close()
    return nc


_NC_CACHE = None


def kernel(**inputs):
    global _NC_CACHE
    x = np.ascontiguousarray(
        np.asarray(inputs["L0_attn_hidden_states"], np.float32).reshape(S, D))
    wq = np.asarray(inputs["L0_attn_wq"], np.float32)
    wkv = np.asarray(inputs["L0_attn_wkv"], np.float32)
    wo = np.ascontiguousarray(np.asarray(inputs["L0_attn_wo"], np.float32))
    km = [np.ascontiguousarray(
        np.asarray(inputs[f"L0_ffn_k{i}"]).astype(np.uint8)) for i in range(3)]
    s3 = np.array([[float(np.asarray(inputs["L0_ffn_s0"])),
                    float(np.asarray(inputs["L0_ffn_s1"])),
                    float(np.asarray(inputs["L0_ffn_s2"]))]], np.float32)

    wq_r = wq.reshape(D, 32, 192)[:, :, :128]
    wk_r = wkv[:, :4096].reshape(D, 32, 128)
    wv_r = wkv[:, 4096:8192].reshape(D, 32, 128)

    def _tile_pm(w):
        return np.ascontiguousarray(
            w.reshape(32, 128, 512).transpose(1, 0, 2).reshape(128, 32 * 512))

    def _tile_blocks(w):
        return np.ascontiguousarray(
            w.reshape(32, 128, 16, 256).transpose(2, 1, 0, 3).reshape(
                2048, 32 * 256))

    wo_t = _tile_blocks(wo)
    km_t = [_tile_blocks(k) for k in km]
    wq_tt = _tile_blocks(np.ascontiguousarray(wq_r.reshape(D, D)))
    wk_tt = _tile_blocks(np.ascontiguousarray(wk_r.reshape(D, D)))
    wv_tt = _tile_blocks(np.ascontiguousarray(wv_r.reshape(D, D)))

    if _NC_CACHE is None:
        _NC_CACHE = build_nc()
    nc = _NC_CACHE

    in_maps = []
    for c in range(NCORES):
        xm = np.concatenate([x[c * CH:(c + 1) * CH],
                             x[(15 - c) * CH:(16 - c) * CH]], 0)
        in_maps.append({
            "x_my": np.ascontiguousarray(xm),
            "wq_t": wq_tt, "wk_t": wk_tt, "wv_t": wv_tt,
            "wo_in": wo_t,
            "k0_in": km_t[0], "k1_in": km_t[1], "k2_in": km_t[2],
            "scal3": s3,
        })

    res = run_bass_kernel_spmd(nc, in_maps, core_ids=list(range(NCORES)))

    y = np.empty((S, D), np.float32)
    for c in range(NCORES):
        ym = np.asarray(res.results[c]["y_my"])
        y[c * CH:(c + 1) * CH] = ym[:CH]
        y[(15 - c) * CH:(16 - c) * CH] = ym[CH:]
    if DEBUG_TAPS:
        kernel.last_results = res
    return y.reshape(1, S, D)
